# version 01 (final)
# baseline (speedup 1.0000x reference)
import contextlib
import numpy as np
import ml_dtypes
import concourse.bass as bass
import concourse.mybir as mybir
from concourse.bass_utils import run_bass_kernel_spmd

F32 = mybir.dt.float32
BF16 = mybir.dt.bfloat16
AF = mybir.ActivationFunctionType
ALU = mybir.AluOpType
AX = mybir.AxisListType

D = 2048
DC = 16
I = 7168
IC = 56
E = 8
R = 64
ALPHA = 16.0 / 64.0
LN_EPS = 1e-5
AT_EPS = 1e-8
SLAB = 256
NSLAB = I // SLAB


class Ctr:
    def __init__(self, es, nc, name):
        self.h = es.enter_context(nc.semaphore(name))
        self.c = 0


class EngPlan:
    def __init__(self):
        self.recs = []

    def w(self, ctr, val=None):
        v = ctr.c if val is None else val
        if v > 0:
            self.recs.append(("w", ctr.h, v))

    def o(self, meth, *a, inc=None, **kw):
        pairs = []
        if inc:
            for ctr, n in inc:
                ctr.c += n
                pairs.append((ctr.h, n))
        self.recs.append(("o", meth, a, kw, pairs))


def _merge_nops(recs):
    out = []
    for r in recs:
        if (r[0] == "o" and r[1] == "nop" and not r[2] and not r[3] and r[4]
                and out and out[-1][0] == "o"
                and out[-1][1] not in ("nop", "dma_start", "collective_compute")):
            prev = out[-1]
            out[-1] = (prev[0], prev[1], prev[2], prev[3], prev[4] + r[4])
        else:
            out.append(r)
    return out


TRACE_RECS = {}


def _runner(recs):
    def run(eng):
        for r in recs:
            if r[0] == "w":
                eng.wait_ge(r[1], r[2])
            else:
                _, meth, a, kw, pairs = r
                instr = getattr(eng, meth)(*a, **kw)
                for h, n in pairs:
                    instr.then_inc(h, n)
    return run


class Prog:
    def __init__(self, T=4096, taps=()):
        self.T = T
        self.NT = T // 128
        self.TQ = min(1024, T)
        self.NQ = T // self.TQ
        self.taps = set(taps)
        self.tapdefs = []
        nc = bass.Bass("TRN2", target_bir_lowering=False, detect_race_conditions=False)
        self.nc = nc
        self.es = contextlib.ExitStack()

        dt = nc.dram_tensor
        self.x = dt("x", [T, D], F32, kind="ExternalInput").ap()
        self.wq_t = dt("wq_t", [D, D], F32, kind="ExternalInput").ap()
        self.wk_t = dt("wk_t", [D, D], F32, kind="ExternalInput").ap()
        self.wv_t = dt("wv_t", [D, D], F32, kind="ExternalInput").ap()
        self.wo_t = dt("wo_t", [D, D], F32, kind="ExternalInput").ap()
        self.brows_in = dt("brows_in", [4, D], F32, kind="ExternalInput").ap()
        self.confw_in = dt("confw_in", [D, E], F32, kind="ExternalInput").ap()
        self.confb_in = dt("confb_in", [1, E], F32, kind="ExternalInput").ap()
        self.wealth_in = dt("wealth_in", [1, E], F32, kind="ExternalInput").ap()
        self.wg_t = dt("wg_t", [D, I], F32, kind="ExternalInput").ap()
        self.wu_t = dt("wu_t", [D, I], F32, kind="ExternalInput").ap()
        self.wd_t = dt("wd_t", [I, D], F32, kind="ExternalInput").ap()
        self.zA_in = dt("zA_in", [D, E, 128], F32, kind="ExternalInput").ap()
        self.guB_in = dt("guB_in", [E, 128, I], F32, kind="ExternalInput").ap()
        self.dAp_in = dt("dAp_in", [I, E, R], F32, kind="ExternalInput").ap()
        self.dBp_in = dt("dBp_in", [4, 128, D], F32, kind="ExternalInput").ap()
        self.cconst_in = dt("cconst_in", [128, 4], F32, kind="ExternalInput").ap()
        self.ident_in = dt("ident_in", [128, 128], F32, kind="ExternalInput").ap()
        self.onesr_in = dt("onesr_in", [1, 128], F32, kind="ExternalInput").ap()
        self.onesc_in = dt("onesc_in", [128, 1], F32, kind="ExternalInput").ap()
        self.yout = dt("yout", [T, D], F32, kind="ExternalOutput").ap()

        self.xn_hbm = dt("xn_hbm", [self.NT, 128, D], BF16).ap()
        self.smdbg_hbm = dt("smdbg_hbm", [128, self.NT * 16], F32).ap()
        self.h_hbm = dt("h_hbm", [IC, 128, T], BF16).ap()
        self.o_hbm = dt("o_hbm", [self.NT, 128, D], BF16).ap()
        self.zd_hbm = dt("zd_hbm", [4, 128, T], BF16).ap()
        self.kvp_hbm = dt("kvp_hbm", [128, 2064], F32).ap()
        self.ktv_hbm = dt("ktv_hbm", [2, 128, 1024], BF16).ap()
        self.cc_in_t = dt("cc_in", [128, 2064], F32)
        self.cc_out_t = dt("cc_out", [128, 2064], F32)

        es = self.es
        self._qsems = {}
        self.sy = Ctr(es, nc, "sy")
        self.gp = Ctr(es, nc, "gp")
        self.tc = Ctr(es, nc, "tc")
        self.vc = Ctr(es, nc, "vc")
        self.sc = Ctr(es, nc, "sc")
        self.cs = Ctr(es, nc, "cs")

        psb = lambda name, shape, dtyp: es.enter_context(nc.sbuf_tensor(name, shape, dtyp))
        self.ident_bf = psb("ident_bf", [128, 128], BF16)
        self.ident_f = psb("ident_f", [128, 128], F32)
        self.brows = psb("brows", [4, D], BF16)
        self.sm = psb("sm", [128, 64], F32)
        self.onesrow_f = psb("onesrow_f", [1, 128], F32)
        self.onesrow_bf = psb("onesrow_bf", [1, 128], BF16)
        self.onescol_bf = psb("onescol_bf", [128, 1], BF16)
        self.confb_row = psb("confb_row", [1, E], F32)
        self.confw_f = psb("confw_f", [128, DC, E], F32)
        self.wealth_rep = psb("wealth_rep", [128, E], F32)
        self.cw_tm = psb("cw_tm", [128, self.NT, E], F32)

        self.block = es.enter_context(nc.Block())
        self._fresh()
        self.build()

    def q(self, name):
        if name not in self._qsems:
            self._qsems[name] = Ctr(self.es, self.nc, "q_" + name)
        return self._qsems[name]

    def _fresh(self):
        self.Y = EngPlan()
        self.G = EngPlan()
        self.Tn = EngPlan()
        self.V = EngPlan()
        self.S = EngPlan()

    def emit(self):
        for key, plan in (("sync", self.Y), ("gpsimd", self.G), ("tensor", self.Tn),
                          ("vector", self.V), ("scalar", self.S)):
            if plan.recs:
                merged = _merge_nops(plan.recs)
                TRACE_RECS.setdefault(key, []).extend(merged)
                getattr(self.block, key)(_runner(merged))
        self._fresh()

    def tapit(self, name, src_ap):
        if name in self.taps:
            out = self.nc.dram_tensor("tap_" + name, list(src_ap.shape), src_ap.dtype,
                                      kind="ExternalOutput").ap()
            self.tapdefs.append((out, src_ap))

    def build(self):
        import os
        upto = os.environ.get("KPHASES", "b2")
        self.ph_consts()
        if upto == "consts":
            self.ph_final()
            return
        self.ph_a()
        if upto != "a":
            self.ph_b1()
        if upto not in ("a", "b1"):
            self.ph_a2()
        if upto not in ("a", "b1", "a2"):
            self.ph_b2()
        self.ph_final()

    def ph_consts(self):
        G, gp = self.G, self.q("c")
        G.o("dma_start", out=self.ident_f[:], in_=self.ident_in[:], inc=[(gp, 16)])
        G.o("dma_start", out=self.ident_bf[:], in_=self.ident_in[:], inc=[(gp, 16)])
        G.o("dma_start", out=self.sm[:, 0:4], in_=self.cconst_in[:], inc=[(gp, 16)])
        G.o("dma_start", out=self.onesrow_f[:], in_=self.onesr_in[:], inc=[(gp, 16)])
        G.o("dma_start", out=self.onesrow_bf[:], in_=self.onesr_in[:], inc=[(gp, 16)])
        G.o("dma_start", out=self.onescol_bf[:], in_=self.onesc_in[:], inc=[(gp, 16)])
        G.o("dma_start", out=self.brows[:], in_=self.brows_in[:], inc=[(gp, 16)])
        G.o("dma_start", out=self.confb_row[:], in_=self.confb_in[:], inc=[(gp, 16)])
        G.o("dma_start",
            out=self.confw_f[:],
            in_=self.confw_in[:].rearrange("(c p) e -> p c e", p=128),
            inc=[(gp, 16)])
        self.gp_consts = gp.c
        self.qc = gp
        self.emit()

    def ph_a(self):
        nc = self.nc
        es = contextlib.ExitStack()
        sb = lambda n, s, d: es.enter_context(nc.sbuf_tensor(n, s, d))
        NT = self.NT
        Y, G, Tn, V, S = self.Y, self.G, self.Tn, self.V, self.S
        sy, gp, tc, vc, sc, cs = self.sy, self.gp, self.tc, self.vc, self.sc, self.cs
        sm = self.sm
        ev = {}

        x_t = sb("x_t", [128, 2, D], F32)
        sqscr = sb("sqscr", [128, 2 * D], F32)
        xn_tm = sb("xn_tm", [128, D], F32)
        xnf32 = sb("xnf32", [128, D], F32)
        xn_bf = sb("xn_bf", [128, 2, D], BF16)
        wealth_sb = sb("wealth_sb", [1, E], F32)
        wkv_bf = sb("wkv_bf", [128, 2, DC, 1024], BF16)
        xn_t = sb("xn_t", [128, 2, D], BF16)
        ktm = sb("ktm", [128, 1024], BF16)
        vtm = sb("vtm", [128, 1024], BF16)
        escr = sb("escr", [128, 2, 512], F32)
        kvpart = sb("kvpart", [128, 2064], F32)
        bkv_r = sb("bkv_r", [1, 2 * D], BF16)
        smdbg = sb("smdbg", [128, 16], F32)
        wubuf = sb("wubuf", [128, 128], F32)

        psTR = es.enter_context(nc.psum_tensor("psTR", [128, 1024], F32))
        psCONF = es.enter_context(nc.psum_tensor("psCONF", [128, 512], F32))
        psKV = es.enter_context(nc.psum_tensor("psKV", [128, 1024], F32))
        psKS = es.enter_context(nc.psum_tensor("psKS", [128, 512], F32))
        psK = es.enter_context(nc.psum_tensor("psK", [128, 1024], F32))

        for P_ in (self.Tn, self.V, self.S, self.Y):
            P_.w(self.qc, self.gp_consts)
        qc = self.qc
        G.o("dma_start", out=bkv_r[0:1, 0:D], in_=self.brows_in[1:2, :], inc=[(qc, 16)])
        G.o("dma_start", out=bkv_r[0:1, D:2 * D], in_=self.brows_in[2:3, :], inc=[(qc, 16)])
        G.o("dma_start", out=wealth_sb[:], in_=self.wealth_in[:], inc=[(qc, 16)])
        wealth_gp = qc.c
        Tn.w(qc, wealth_gp)
        Tn.o("matmul", psKS[:, 16:16 + E], self.onesrow_f[:], wealth_sb[:],
             start=True, stop=True)
        Tn.o("nop", inc=[(tc, 1)])
        ev["wealth_t"] = tc.c
        V.w(tc, ev["wealth_t"])
        V.o("tensor_copy", self.wealth_rep[:], psKS[:, 16:16 + E], inc=[(vc, 1)])
        self.wealth_v = vc.c

        import os as _os
        LNP = _os.environ.get("LNPART", "all")
        tile_order = list(range(NT)) + [0]
        for i in tile_order:
            qx = self.q("x%d" % (i % 2))
            if i >= 2:
                Y.w(vc, ev[("xn", i - 2)])
                Y.w(sc, ev[("sq", i - 2)])
            Y.o("dma_start", out=x_t[:, i % 2], in_=self.x[i * 128:(i + 1) * 128, :],
                inc=[(qx, 16)])
            ev[("xload", i)] = (qx, qx.c)

            S.w(*ev[("xload", i)])
            if i == 0:
                qwu = self.q("wu")
                Y.o("dma_start", out=wubuf[:], in_=self.x[0:128, 0:128],
                    inc=[(qwu, 16)])
                S.w(qwu, 16)
                V.w(qwu, 16)
            if i >= 2:
                S.w(vc, ev[("sqred", i - 2)])
            S.o("activation", out=sqscr[:, (i % 2) * D:(i % 2 + 1) * D],
                in_=x_t[:, i % 2], func=AF.Square, inc=[(sc, 1)])
            ev[("sq", i)] = sc.c

            V.w(*ev[("xload", i)])
            V.o("reduce_sum", sm[:, 4:5], x_t[:, i % 2], axis=AX.X, inc=[(vc, 1)])
            V.w(vc)
            V.o("tensor_scalar", sm[:, 5:6], sm[:, 4:5], 1.0 / D, None, op0=ALU.mult,
                inc=[(vc, 1)])
            V.w(vc)
            V.w(sc, ev[("sq", i)])
            V.o("reduce_sum", sm[:, 6:7], sqscr[:, (i % 2) * D:(i % 2 + 1) * D],
                axis=AX.X, inc=[(vc, 1)])
            ev[("sqred", i)] = vc.c
            V.w(vc)
            V.o("tensor_scalar", sm[:, 7:8], sm[:, 6:7], 1.0 / D, None, op0=ALU.mult)
            V.o("tensor_tensor", sm[:, 8:9], sm[:, 5:6], sm[:, 5:6], op=ALU.mult,
                inc=[(vc, 1)])
            V.w(vc)
            V.o("tensor_tensor", sm[:, 9:10], sm[:, 7:8], sm[:, 8:9], op=ALU.subtract,
                inc=[(vc, 1)])
            ev[("var", i)] = vc.c

            S.w(vc, ev[("var", i)])
            S.o("activation", out=sm[:, 10:11], in_=sm[:, 9:10], func=AF.Ln,
                bias=sm[:, 2:3], scale=1.0, inc=[(sc, 1)])
            S.w(sc)
            S.o("activation", out=sm[:, 10:11], in_=sm[:, 10:11], func=AF.Exp,
                scale=-0.5, inc=[(sc, 1)])
            ev[("rstd", i)] = sc.c

            V.w(sc, ev[("rstd", i)])
            V.o("tensor_scalar", xn_tm[:], x_t[:, i % 2], sm[:, 5:6], sm[:, 10:11],
                op0=ALU.subtract, op1=ALU.mult, inc=[(vc, 1)])
            ev[("xn", i)] = vc.c

            if LNP == "tm":
                V.o("tensor_copy", xn_bf[:, i % 2, :], xn_tm[:], inc=[(vc, 1)])
                ev[("xnf32", i, 0)] = vc.c
                ev[("xnf32", i, 1)] = vc.c
                ev[("tr", i, 0)] = tc.c
                ev[("tr", i, 1)] = tc.c
                S.w(vc, vc.c)
                S.o("nop", inc=[(sc, 1)])
                S.o("nop", inc=[(sc, 1)])
                ev[("xnbf", i, 0)] = sc.c
                ev[("xnbf", i, 1)] = sc.c
            if _os.environ.get("KSTATS") == "1":
                V.o("tensor_copy", smdbg[:], sm[:, 0:16], inc=[(vc, 1)])
                Y.w(vc, self.vc.c)
                Y.o("dma_start", out=self.smdbg_hbm[:, i * 16:(i + 1) * 16],
                    in_=smdbg[:], inc=[(sy, 16)])
                ev[("smdbg", i)] = self.sy.c
                V.w(self.sy, ev[("smdbg", i)])
                V.o("nop", inc=[(vc, 1)])
            for hf in range(2 if LNP in ("all", "tr", "conf") else 0):
                Tn.w(vc, ev[("xn", i)])
                if i > 0 or hf > 0:
                    pk = (i, hf - 1) if hf > 0 else (i - 1, 1)
                    Tn.w(sc, ev[("xnbf",) + pk])
                    Tn.w(vc, ev[("xnf32",) + pk])
                for c in range(8):
                    cc = hf * 8 + c
                    Tn.o("transpose", psTR[:, c * 128:(c + 1) * 128],
                         xn_tm[:, cc * 128:(cc + 1) * 128], self.ident_f[:])
                Tn.o("nop", inc=[(tc, 1)])
                ev[("tr", i, hf)] = tc.c

                S.w(tc, ev[("tr", i, hf)])
                if i >= 2 and hf == 0:
                    S.w(*ev[("xnstore", i - 2)])
                S.o("activation", out=xn_bf[:, i % 2, hf * 1024:(hf + 1) * 1024],
                    in_=psTR[:], func=AF.Copy, inc=[(sc, 1)])
                ev[("xnbf", i, hf)] = sc.c

                V.w(tc, ev[("tr", i, hf)])
                V.w(sc, ev[("xnbf", i, hf)])
                if i > 0 and hf == 0:
                    V.w(tc, ev[("conf", i - 1)])
                V.o("tensor_copy", xnf32[:, hf * 1024:(hf + 1) * 1024], psTR[:],
                    inc=[(vc, 1)])
                ev[("xnf32", i, hf)] = vc.c

            if LNP not in ("all", "conf"):
                ev[("conf", i)] = tc.c
                ev[("confev", i)] = vc.c
                ev[("xnstore", i)] = sy.c
                ev[("xnbf", i, 1)] = sc.c if LNP == "stats" else ev[("xnbf", i, 1)]
                continue
            Tn.w(vc, ev[("xnf32", i, 1)])
            if i > 0:
                Tn.w(vc, ev[("confev", i - 1)])
            for c in range(DC):
                Tn.o("matmul", psCONF[:, 0:E], xnf32[:, c * 128:(c + 1) * 128],
                     self.confw_f[:, c, :], start=(c == 0), stop=False)
            Tn.o("matmul", psCONF[:, 0:E], self.onesrow_f[:], self.confb_row[:],
                 start=False, stop=True)
            Tn.o("nop", inc=[(tc, 1)])
            ev[("conf", i)] = tc.c

            V.w(tc, ev[("conf", i)])
            V.o("tensor_copy", self.cw_tm[:, i, :], psCONF[:, 0:E], inc=[(vc, 1)])
            ev[("confev", i)] = vc.c

            qsx = self.q("sx%d" % (i % 2))
            Y.w(sc, ev[("xnbf", i, 1)])
            Y.o("dma_start", out=self.xn_hbm[i], in_=xn_bf[:, i % 2], inc=[(qsx, 16)])
            ev[("xnstore", i)] = (qsx, qsx.c)

        import os
        if os.environ.get("KPHASES", "") == "ln":
            self.ev_a = ev
            self.emit()
            es.close()
            return
        for hh in range(2):
            if hh == 1:
                G.w(tc, ev[("kvhalf_t", 0)])
            qkv = self.q("wkv")
            G.o("dma_start",
                out=wkv_bf[:, 0],
                in_=self.wk_t[:, hh * 1024:(hh + 1) * 1024].rearrange(
                    "(c p) f -> p c f", p=128), inc=[(qkv, 16)])
            G.o("dma_start",
                out=wkv_bf[:, 1],
                in_=self.wv_t[:, hh * 1024:(hh + 1) * 1024].rearrange(
                    "(c p) f -> p c f", p=128), inc=[(qkv, 16)])
            ev[("wkv", hh)] = (qkv, qkv.c)
            if hh == 1:
                Tn.w(vc, ev[("kvev", 0)])

            for i in range(NT):
                if hh == 0 and i < 2:
                    pass
                else:
                    pk = ("xnt_used", hh, i - 2) if i >= 2 else ("kvhalf_t", hh - 1)
                    Y.w(tc, ev[pk])
                Y.w(*ev[("xnstore", i)])
                qxn = self.q("xn%d" % (i % 2))
                Y.o("dma_start", out=xn_t[:, i % 2], in_=self.xn_hbm[i], inc=[(qxn, 16)])
                ev[("xnt", hh, i)] = (qxn, qxn.c)

                Tn.w(*ev[("xnt", hh, i)])
                Tn.w(*ev[("wkv", hh)])
                for kv in range(2):
                    ps = psK[:, kv * 512:(kv + 1) * 512]
                    tmb = (ktm, vtm)[kv]
                    for fg in range(2):
                        prevk = None
                        if fg == 1:
                            prevk = ("kcons", hh, i, kv, 0)
                        elif i > 0:
                            prevk = ("kcons", hh, i - 1, kv, 1)
                        elif hh == 1:
                            prevk = ("kcons", 0, NT - 1, kv, 1)
                        if prevk:
                            Tn.w(vc, ev[prevk])
                        for c in range(DC):
                            Tn.o("matmul", ps,
                                 xn_t[:, i % 2, c * 128:(c + 1) * 128],
                                 wkv_bf[:, kv, c, fg * 512:(fg + 1) * 512],
                                 start=(c == 0), stop=False)
                        Tn.o("matmul", ps, self.onesrow_bf[:],
                             bkv_r[0:1, kv * D + hh * 1024 + fg * 512:
                                   kv * D + hh * 1024 + (fg + 1) * 512],
                             start=False, stop=True)
                        Tn.o("nop", inc=[(tc, 1)])
                        ev[("kvmm", hh, i, kv, fg)] = tc.c

                        V.w(tc, ev[("kvmm", hh, i, kv, fg)])
                        if i > 0:
                            V.w(tc, ev[("xnt_used", hh, i - 1)])
                        elif hh == 1:
                            V.w(tc, ev[("kvhalf_t", 0)])
                        if kv == 0:
                            V.o("tensor_scalar", escr[:, 0], ps, sm[:, 0:1], None,
                                op0=ALU.min, inc=[(vc, 1)])
                            ev[("kmin", hh, i, fg)] = vc.c
                            S.w(vc, ev[("kmin", hh, i, fg)])
                            S.o("activation", out=escr[:, 1], in_=escr[:, 0], func=AF.Exp,
                                inc=[(sc, 1)])
                            ev[("kexp", hh, i, fg)] = sc.c
                            V.w(sc, ev[("kexp", hh, i, fg)])
                            V.o("scalar_tensor_tensor",
                                tmb[:, fg * 512:(fg + 1) * 512], ps, sm[:, 0:1],
                                escr[:, 1], op0=ALU.max, op1=ALU.add, inc=[(vc, 1)])
                        else:
                            V.o("tensor_copy", tmb[:, fg * 512:(fg + 1) * 512], ps,
                                inc=[(vc, 1)])
                        ev[("kcons", hh, i, kv, fg)] = vc.c

                Tn.w(vc, ev[("kcons", hh, i, 0, 1)])
                Tn.w(vc, ev[("kcons", hh, i, 1, 1)])
                for h in range(8):
                    Tn.o("matmul", psKV[:, h * 128:(h + 1) * 128],
                         ktm[:, h * 128:(h + 1) * 128], vtm[:, h * 128:(h + 1) * 128],
                         start=(i == 0 and h % 4 == 0),
                         stop=(i == NT - 1 and h % 4 == 3))
                for h in range(8):
                    Tn.o("matmul", psKS[:, hh * 8 + h:hh * 8 + h + 1],
                         ktm[:, h * 128:(h + 1) * 128], self.onescol_bf[:],
                         start=(i == 0 and hh == 0 and h == 0),
                         stop=(i == NT - 1 and hh == 1 and h == 7))
                Tn.o("nop", inc=[(tc, 1)])
                ev[("xnt_used", hh, i)] = tc.c
            ev[("kvhalf_t", hh)] = tc.c

            V.w(tc, ev[("kvhalf_t", hh)])
            V.o("tensor_copy", kvpart[:, hh * 1024:(hh + 1) * 1024], psKV[:],
                inc=[(vc, 1)])
            ev[("kvev", hh)] = vc.c

        V.o("tensor_copy", kvpart[:, 2048:2064], psKS[:, 0:16], inc=[(vc, 1)])
        ev["ksumev"] = vc.c

        G.w(vc, ev[("kvev", 1)])
        G.w(vc, ev["ksumev"])
        qcc = self.q("cc")
        G.o("dma_start", out=self.kvp_hbm, in_=kvpart[:], inc=[(qcc, 16)])
        G.o("dma_start", out=self.cc_in_t.ap(), in_=kvpart[:], inc=[(qcc, 16)])
        G.w(qcc, qcc.c)
        qdbg = self.q("dbg")
        G.o("dma_start", out=self.ktv_hbm[0], in_=ktm[:], inc=[(qdbg, 16)])
        G.o("dma_start", out=self.ktv_hbm[1], in_=vtm[:], inc=[(qdbg, 16)])
        if os.environ.get("KPHASES", "") == "akv":
            self.ev_a = ev
            self.emit()
            es.close()
            return
        G.o("collective_compute", "AllReduce", ALU.add,
            replica_groups=[[0, 1, 2, 3], [4, 5, 6, 7]],
            ins=[self.cc_in_t.ap().opt()], outs=[self.cc_out_t.ap().opt()],
            inc=[(cs, 1)])

        self.ev_a = ev
        self.emit()
        es.close()

    def ph_b1(self):
        nc = self.nc
        es = contextlib.ExitStack()
        sb = lambda n, s, d: es.enter_context(nc.sbuf_tensor(n, s, d))
        T, TQ, NQ = self.T, self.TQ, self.NQ
        NTC = TQ // 512
        NIC = SLAB // 128
        Y, G, Tn, V, S = self.Y, self.G, self.Tn, self.V, self.S
        sy, gp, tc, vc, sc = self.sy, self.gp, self.tc, self.vc, self.sc
        sm = self.sm
        ev = {}
        evA = self.ev_a

        zA_bf = sb("zA_bf", [128, DC, E * 128], BF16)
        xn_q = sb("xn_q", [128, TQ // 128, D], BF16)
        z_gu = sb("z_gu", [128, E, TQ], BF16)
        w_rep = sb("w_rep", [128, E, TQ], BF16)
        wfm_q = sb("wfm_q", [E, TQ], F32)
        zd_acc = sb("zd_acc", [128, 4, TQ], F32)
        zd_bf = sb("zd_bf", [128, 4, TQ], BF16)
        wslab = sb("wslab", [128, 2, 2 * DC, SLAB], BF16)
        gub_s = sb("gub_s", [128, 2, E, SLAB], BF16)
        da_s = sb("da_s", [128, 2, NIC, E * R], BF16)
        bgbf = sb("bgbf", [128, 2, 512], BF16)
        bubf = sb("bubf", [128, 2, 512], BF16)
        sbf = sb("sbf", [128, 2, 512], BF16)
        pbf = sb("pbf", [128, 2, 512], BF16)
        qbf = sb("qbf", [128, 2, 512], BF16)
        hacc = sb("hacc", [128, 2, 512], F32)
        hstage = sb("hstage", [128, 2, 512], BF16)
        scr8 = sb("scr8", [128, 6, E], F32)
        w8s = sb("w8s", [8, 128], F32)
        wrow = sb("wrow", [1, E * 128], F32)

        ps = es.enter_context(nc.psum_tensor("ps_b1", [128, 4096], F32))
        bank = lambda i: ps[:, i * 512:(i + 1) * 512]

        qza = self.q("za")
        G.o("dma_start",
            out=zA_bf[:],
            in_=self.zA_in[:].rearrange("(c p) e m -> p c (e m)", p=128),
            inc=[(qza, 16)])
        ev["zA"] = (qza, qza.c)

        free = {
            "sbf": [0, 0], "pq": [0, 0], "qbf": [0, 0], "hacc": [0, 0],
            "hstage": [0, 0], "bgbf": [0, 0], "lg": 0, "lu": 0, "base": 0,
        }
        last_xnq_used = 0
        wfmev_prev = 0

        for q in range(NQ):
            qoff = q * TQ
            qt0 = q * (TQ // 128)

            qxq = self.q("xq")
            for tb in range(TQ // 128):
                if q > 0 and tb == 0:
                    Y.w(tc, last_xnq_used)
                Y.w(*evA[("xnstore", qt0 + tb)])
                Y.o("dma_start", out=xn_q[:, tb, :], in_=self.xn_hbm[qt0 + tb],
                    inc=[(qxq, 16)])
            ev[("xnq", q)] = (qxq, qxq.c)

            prev_top2 = 0
            for j in range(TQ // 128):
                i = qt0 + j
                lg = self.cw_tm[:, i, :]
                S.w(vc, evA[("confev", i)])
                S.w(vc, prev_top2)
                S.o("activation", out=scr8[:, 0, :], in_=lg, func=AF.Exp, scale=-1.0,
                    inc=[(sc, 1)])
                S.w(sc)
                S.o("activation", out=scr8[:, 0, :], in_=scr8[:, 0, :], func=AF.Ln,
                    bias=sm[:, 1:2], scale=1.0, inc=[(sc, 1)])
                S.w(sc)
                S.o("activation", out=scr8[:, 0, :], in_=scr8[:, 0, :], func=AF.Exp,
                    scale=-1.0, inc=[(sc, 1)])
                ev[("sig", i)] = sc.c
                V.w(sc, ev[("sig", i)])
                V.w(vc, self.wealth_v)
                V.o("tensor_tensor", scr8[:, 1, :], scr8[:, 0, :], self.wealth_rep[:],
                    op=ALU.mult, inc=[(vc, 1)])
                V.w(vc)
                V.o("reduce_max", sm[:, 11:12], scr8[:, 1, :], axis=AX.X, inc=[(vc, 1)])
                V.w(vc)
                V.o("tensor_scalar", scr8[:, 2, :], scr8[:, 1, :], sm[:, 11:12], None,
                    op0=ALU.is_equal, inc=[(vc, 1)])
                V.w(vc)
                V.o("scalar_tensor_tensor", scr8[:, 3, :], scr8[:, 2, :], -1e30,
                    scr8[:, 1, :], op0=ALU.mult, op1=ALU.add, inc=[(vc, 1)])
                V.w(vc)
                V.o("reduce_max", sm[:, 12:13], scr8[:, 3, :], axis=AX.X, inc=[(vc, 1)])
                V.w(vc)
                V.o("tensor_scalar", scr8[:, 4, :], scr8[:, 3, :], sm[:, 12:13], None,
                    op0=ALU.is_equal, inc=[(vc, 1)])
                V.w(vc)
                V.o("tensor_tensor", sm[:, 13:14], sm[:, 12:13], sm[:, 11:12],
                    op=ALU.subtract, inc=[(vc, 1)])
                ev[("top2", i)] = vc.c
                prev_top2 = vc.c
                S.w(vc, ev[("top2", i)])
                S.o("activation", out=sm[:, 14:15], in_=sm[:, 13:14], func=AF.Exp,
                    inc=[(sc, 1)])
                S.w(sc)
                S.o("activation", out=sm[:, 14:15], in_=sm[:, 14:15], func=AF.Ln,
                    bias=sm[:, 1:2], inc=[(sc, 1)])
                S.w(sc)
                S.o("activation", out=sm[:, 15:16], in_=sm[:, 14:15], func=AF.Exp,
                    scale=-1.0, inc=[(sc, 1)])
                ev[("w1", i)] = sc.c
                V.w(sc, ev[("w1", i)])
                V.o("tensor_tensor", sm[:, 16:17], sm[:, 1:2], sm[:, 15:16],
                    op=ALU.subtract)
                V.o("tensor_scalar", scr8[:, 5, :], scr8[:, 2, :], sm[:, 15:16], None,
                    op0=ALU.mult, inc=[(vc, 1)])
                V.w(vc)
                V.o("scalar_tensor_tensor", lg, scr8[:, 4, :], sm[:, 16:17],
                    scr8[:, 5, :], op0=ALU.mult, op1=ALU.add, inc=[(vc, 1)])
                ev[("W", i)] = vc.c

            wrep_prev = 0
            wrow_prev = 0
            wrep_first = None
            for j in range(TQ // 128):
                i = qt0 + j
                Tn.w(vc, ev[("W", i)])
                Tn.w(vc, wfmev_prev)
                Tn.o("transpose", ps[0:8, 0:128], self.cw_tm[:, i, :], self.ident_f[:])
                Tn.o("nop", inc=[(tc, 1)])
                ev[("wfmtr", i)] = tc.c
                qwr = self.q("wr")
                V.w(tc, ev[("wfmtr", i)])
                V.w(qwr, wrow_prev)
                V.o("tensor_copy", w8s[:], ps[0:8, 0:128], inc=[(vc, 1)])
                ev[("wfmev", i)] = vc.c
                wfmev_prev = vc.c
                Y.w(vc, ev[("wfmev", i)])
                if wrep_first is not None:
                    Y.w(tc, wrep_first)
                Y.o("dma_start", out=wrow[0:1, :].rearrange("o (e t) -> o e t", e=E),
                    in_=w8s[:], inc=[(qwr, 16)])
                ev[("wrow", i)] = (qwr, qwr.c)
                wrow_prev = qwr.c
                Tn.w(*ev[("wrow", i)])
                Tn.w(sc, wrep_prev)
                for e in range(E):
                    Tn.o("matmul", ps[:, 512 + e * 128:512 + (e + 1) * 128],
                         self.onesrow_f[:], wrow[0:1, e * 128:(e + 1) * 128],
                         start=(e % 4 == 0), stop=(e % 4 == 3))
                Tn.o("nop", inc=[(tc, 1)])
                ev[("wrepmm", i)] = tc.c
                wrep_first = tc.c
                S.w(tc, ev[("wrepmm", i)])
                S.o("activation",
                    out=w_rep[:, :, j * 128:(j + 1) * 128],
                    in_=ps[:, 512:1536].rearrange("p (e t) -> p e t", e=E),
                    func=AF.Copy, inc=[(sc, 1)])
                wrep_prev = sc.c
            ev[("wrep", q)] = sc.c

            zev_prev = 0
            for e in range(E):
                for tcc in range(NTC):
                    Tn.w(*ev["zA"])
                    Tn.w(*ev[("xnq", q)])
                    Tn.w(vc, wfmev_prev)
                    Tn.w(sc, zev_prev)
                    for c in range(DC):
                        Tn.o("matmul", bank(0),
                             zA_bf[:, c, e * 128:(e + 1) * 128],
                             xn_q[:, tcc * 4:(tcc + 1) * 4, c * 128:(c + 1) * 128],
                             start=(c == 0), stop=(c == DC - 1))
                    Tn.o("nop", inc=[(tc, 1)])
                    ev[("zmm", q, e, tcc)] = tc.c
                    S.w(tc, ev[("zmm", q, e, tcc)])
                    S.o("activation", out=z_gu[:, e, tcc * 512:(tcc + 1) * 512],
                        in_=bank(0), func=AF.Copy, scale=ALPHA, inc=[(sc, 1)])
                    zev_prev = sc.c
            ev[("zprep", q)] = sc.c

            for s in range(NSLAB):
                b = s % 2
                s0 = s * SLAB
                qw = self.q("w%d" % b)
                G.w(tc, ev.get(("slabcons", q, s - 2),
                               ev.get(("slabcons", q - 1, NSLAB - 2 + s), 0)))
                G.o("dma_start",
                    out=wslab[:, b, 0:DC, :],
                    in_=self.wg_t[:, s0:s0 + SLAB].rearrange("(c p) i -> p c i", p=128),
                    inc=[(qw, 16)])
                G.o("dma_start",
                    out=wslab[:, b, DC:2 * DC, :],
                    in_=self.wu_t[:, s0:s0 + SLAB].rearrange("(c p) i -> p c i", p=128),
                    inc=[(qw, 16)])
                G.o("dma_start",
                    out=gub_s[:, b],
                    in_=self.guB_in[:, :, s0:s0 + SLAB].rearrange("e p i -> p e i"),
                    inc=[(qw, 16)])
                G.o("dma_start",
                    out=da_s[:, b],
                    in_=self.dAp_in[s0:s0 + SLAB].rearrange("(i p) e r -> p i (e r)", p=128),
                    inc=[(qw, 16)])
                ev[("wload", q, s)] = (qw, qw.c)

                for tcc in range(NTC):
                    for icc in range(NIC):
                        Tn.w(*ev[("wload", q, s)])
                        Tn.w(sc, free["base"])
                        if (s, tcc, icc) == (0, 0, 0):
                            Tn.w(sc, ev[("wrep", q)])
                            Tn.w(sc, ev[("zprep", q)])
                        for c in range(DC):
                            Tn.o("matmul", bank(0),
                                 wslab[:, b, c, icc * 128:(icc + 1) * 128],
                                 xn_q[:, tcc * 4:(tcc + 1) * 4, c * 128:(c + 1) * 128],
                                 start=(c == 0), stop=(c == DC - 1))
                        for c in range(DC):
                            Tn.o("matmul", bank(1),
                                 wslab[:, b, DC + c, icc * 128:(icc + 1) * 128],
                                 xn_q[:, tcc * 4:(tcc + 1) * 4, c * 128:(c + 1) * 128],
                                 start=(c == 0), stop=(c == DC - 1))
                        Tn.o("nop", inc=[(tc, 1)])
                        ev[("base", q, s, tcc, icc)] = tc.c

                        S.w(tc, ev[("base", q, s, tcc, icc)])
                        S.w(tc, free["bgbf"][icc])
                        S.o("activation", out=bgbf[:, icc], in_=bank(0), func=AF.Copy)
                        S.o("activation", out=bubf[:, icc], in_=bank(1), func=AF.Copy,
                            inc=[(sc, 1)])
                        ev[("baseev", q, s, tcc, icc)] = sc.c
                        free["base"] = sc.c

                        for e in range(E):
                            if e == 0:
                                Tn.w(sc, ev[("baseev", q, s, tcc, icc)])
                            Tn.w(sc, free["lg"])
                            Tn.w(vc, free["lu"])
                            Tn.o("matmul", bank(2),
                                 gub_s[0:64, b, e, icc * 128:(icc + 1) * 128],
                                 z_gu[0:64, e, tcc * 512:(tcc + 1) * 512],
                                 start=True, stop=False, tile_position=(0, 0))
                            Tn.o("matmul", bank(3),
                                 gub_s[64:128, b, e, icc * 128:(icc + 1) * 128],
                                 z_gu[64:128, e, tcc * 512:(tcc + 1) * 512],
                                 start=True, stop=False, tile_position=(64, 0))
                            Tn.o("matmul", bank(2), self.ident_bf[:], bgbf[:, icc],
                                 start=False, stop=True)
                            Tn.o("matmul", bank(3), self.ident_bf[:], bubf[:, icc],
                                 start=False, stop=True)
                            Tn.o("nop", inc=[(tc, 1)])
                            ev[("lora", q, s, tcc, icc, e)] = tc.c
                            if e == E - 1:
                                free["bgbf"][icc] = tc.c

                            S.w(tc, ev[("lora", q, s, tcc, icc, e)])
                            S.w(vc, free["sbf"][e % 2])
                            S.o("activation", out=sbf[:, e % 2], in_=bank(2), func=AF.Silu,
                                inc=[(sc, 1)])
                            ev[("silu", q, s, tcc, icc, e)] = sc.c
                            free["lg"] = sc.c

                            V.w(sc, ev[("silu", q, s, tcc, icc, e)])
                            V.o("tensor_tensor", pbf[:, e % 2], sbf[:, e % 2], bank(3),
                                op=ALU.mult, inc=[(vc, 1)])
                            ev[("p", q, s, tcc, icc, e)] = vc.c
                            free["lu"] = vc.c
                            free["sbf"][e % 2] = vc.c
                            V.w(vc)
                            V.w(tc, free["qbf"][e % 2])
                            if e == 0:
                                V.w(sc, free["hacc"][icc])
                            V.o("tensor_tensor", qbf[:, e % 2], pbf[:, e % 2],
                                w_rep[:, e, tcc * 512:(tcc + 1) * 512], op=ALU.mult,
                                inc=[(vc, 1)])
                            ev[("qq", q, s, tcc, icc, e)] = vc.c
                            V.w(vc)
                            if e == 0:
                                V.o("tensor_copy", hacc[:, icc], qbf[:, e % 2], inc=[(vc, 1)])
                            else:
                                V.o("tensor_tensor", hacc[:, icc], hacc[:, icc], qbf[:, e % 2],
                                    op=ALU.add, inc=[(vc, 1)])
                            ev[("h", q, s, tcc, icc, e)] = vc.c

                            Tn.w(vc, ev[("qq", q, s, tcc, icc, e)])
                            if icc == 0 and e < 2:
                                if tcc > 0:
                                    Tn.w(vc, ev[("zdev", q, s, tcc - 1)])
                                elif s > 0:
                                    Tn.w(vc, ev[("zdev", q, s - 1, NTC - 1)])
                                elif q > 0:
                                    Tn.w(vc, ev[("zdev", q - 1, NSLAB - 1, NTC - 1)])
                            pr = e // 2
                            half = e % 2
                            Tn.o("matmul",
                                 ps[half * 64:half * 64 + 64,
                                    (4 + pr) * 512:(5 + pr) * 512],
                                 da_s[:, b, icc, e * R:(e + 1) * R], qbf[:, e % 2],
                                 start=(icc == 0), stop=(icc == NIC - 1),
                                 tile_position=(0, half * 64))
                            Tn.o("nop", inc=[(tc, 1)])
                            ev[("zd", q, s, tcc, icc, e)] = tc.c
                            free["qbf"][e % 2] = tc.c

                        qhs = self.q("hs%d" % icc)
                        S.w(vc, ev[("h", q, s, tcc, icc, E - 1)])
                        S.w(qhs, free["hstage"][icc])
                        S.o("activation", out=hstage[:, icc], in_=hacc[:, icc], func=AF.Copy,
                            inc=[(sc, 1)])
                        ev[("hcast", q, s, tcc, icc)] = sc.c
                        free["hacc"][icc] = sc.c
                        Y.w(sc, ev[("hcast", q, s, tcc, icc)])
                        Y.o("dma_start",
                            out=self.h_hbm[2 * s + icc, :,
                                           qoff + tcc * 512:qoff + (tcc + 1) * 512],
                            in_=hstage[:, icc], inc=[(qhs, 16)])
                        ev[("hstore", q, s, tcc, icc)] = qhs.c
                        free["hstage"][icc] = qhs.c

                    V.w(tc, ev[("zd", q, s, tcc, NIC - 1, E - 1)])
                    for pr in range(4):
                        src = ps[:, (4 + pr) * 512:(5 + pr) * 512]
                        dst = zd_acc[:, pr, tcc * 512:(tcc + 1) * 512]
                        if s == 0:
                            V.o("tensor_copy", dst, src)
                        else:
                            V.o("tensor_tensor", dst, dst, src, op=ALU.add)
                    V.o("nop", inc=[(vc, 1)])
                    ev[("zdev", q, s, tcc)] = vc.c
                ev[("slabcons", q, s)] = tc.c

            qzd = self.q("zd")
            S.w(vc, ev[("zdev", q, NSLAB - 1, NTC - 1)])
            if q > 0:
                S.w(qzd, ev[("zdstore", q - 1)])
            for pr in range(4):
                S.o("activation", out=zd_bf[:, pr, :], in_=zd_acc[:, pr, :],
                    func=AF.Copy, scale=ALPHA)
            S.o("nop", inc=[(sc, 1)])
            ev[("zdcast", q)] = sc.c
            Y.w(sc, ev[("zdcast", q)])
            for pr in range(4):
                Y.o("dma_start", out=self.zd_hbm[pr, :, qoff:qoff + TQ],
                    in_=zd_bf[:, pr, :], inc=[(qzd, 16)])
            ev[("zdstore", q)] = qzd.c
            last_xnq_used = tc.c

        self.ev_b1 = ev
        self.b1_end_sy = self.sy.c
        self.b1_end_tc = self.tc.c
        self.emit()
        es.close()

    def ph_a2(self):
        nc = self.nc
        es = contextlib.ExitStack()
        sb = lambda n, s, d: es.enter_context(nc.sbuf_tensor(n, s, d))
        NT = self.NT
        Y, G, Tn, V, S = self.Y, self.G, self.Tn, self.V, self.S
        sy, gp, tc, vc, sc, cs = self.sy, self.gp, self.tc, self.vc, self.sc, self.cs
        sm = self.sm
        ev = {}
        evA = self.ev_a

        wq_bf = sb("wq_bf", [128, DC, D], BF16)
        kvtot = sb("kvtot", [128, 2064], F32)
        kv_bf = sb("kv_bf", [128, D], BF16)
        ksrows = sb("ksrows", [16, 128], F32)
        ksflat = sb("ksflat", [1, D], F32)
        ksrep = sb("ksrep", [128, D], F32)
        xn_t = sb("xn_t2", [128, 2, D], BF16)
        qtm = sb("qtm", [128, D], BF16)
        qptm = sb("qptm", [128, D], BF16)
        qpfm = sb("qpfm", [128, D], BF16)
        tmpf = sb("tmpf", [128, D], F32)
        dn16 = sb("dn16", [128, 16], F32)
        rcp16 = sb("rcp16", [128, 16], F32)
        escr = sb("escr2", [128, 2, 512], F32)
        o_buf = sb("o_buf", [128, 2, D], BF16)

        psQ = es.enter_context(nc.psum_tensor("psQ", [128, 2048], F32))
        psT = es.enter_context(nc.psum_tensor("psT", [128, 1024], BF16))
        psW = es.enter_context(nc.psum_tensor("psW", [128, 512], F32))

        qbig = self.q("big")
        qkt = self.q("kt")
        G.w(tc, self.b1_end_tc)
        G.o("dma_start",
            out=wq_bf[:],
            in_=self.wq_t[:].rearrange("(c p) f -> p c f", p=128), inc=[(qbig, 16)])
        ev["wq"] = (qbig, qbig.c)
        Y.w(tc, self.b1_end_tc)
        G.w(cs, self.cs.c)
        G.o("dma_start", out=kvtot[:], in_=self.cc_out_t.ap(), inc=[(qkt, 16)])
        ev["kvtot"] = (qkt, qkt.c)

        V.w(*ev["kvtot"])
        V.o("tensor_copy", kv_bf[:], kvtot[:, 0:2048], inc=[(vc, 1)])
        ev["kvbf"] = vc.c

        Tn.w(*ev["kvtot"])
        Tn.o("transpose", psW[0:16, 0:128], kvtot[:, 2048:2064], self.ident_f[:])
        Tn.o("nop", inc=[(tc, 1)])
        ev["kstr"] = tc.c
        V.w(tc, ev["kstr"])
        V.o("tensor_copy", ksrows[:], psW[0:16, 0:128], inc=[(vc, 1)])
        ev["ksrows"] = vc.c
        qks = self.q("ks")
        Y.w(vc, ev["ksrows"])
        Y.o("dma_start", out=ksflat[0:1, :].rearrange("o (h d) -> o h d", h=16),
            in_=ksrows[:], inc=[(qks, 16)])
        ev["ksflat"] = (qks, qks.c)
        Tn.w(*ev["ksflat"])
        for fg in range(4):
            Tn.o("matmul", psQ[:, fg * 512:(fg + 1) * 512], self.onesrow_f[:],
                 ksflat[0:1, fg * 512:(fg + 1) * 512], start=True, stop=True)
        Tn.o("nop", inc=[(tc, 1)])
        ev["ksrepmm"] = tc.c
        V.w(tc, ev["ksrepmm"])
        V.o("tensor_copy", ksrep[:, 0:1024], psQ[:, 0:1024])
        V.o("tensor_copy", ksrep[:, 1024:2048], psQ[:, 1024:2048], inc=[(vc, 1)])
        ev["ksrep"] = vc.c

        for i in range(NT):
            qxn = self.q("xn%d" % (i % 2))
            if i >= 2:
                Y.w(tc, ev[("qmm", i - 2)])
            Y.o("dma_start", out=xn_t[:, i % 2], in_=self.xn_hbm[i], inc=[(qxn, 16)])
            ev[("xnt", i)] = (qxn, qxn.c)

            Tn.w(*ev[("xnt", i)])
            Tn.w(*ev["wq"])
            if i == 0:
                Tn.w(vc, ev["ksrep"])
            else:
                Tn.w(sc, ev[("oev", i - 1)])
            for fg in range(4):
                for c in range(DC):
                    Tn.o("matmul", psQ[:, fg * 512:(fg + 1) * 512],
                         xn_t[:, i % 2, c * 128:(c + 1) * 128],
                         wq_bf[:, c, fg * 512:(fg + 1) * 512],
                         start=(c == 0), stop=False)
                Tn.o("matmul", psQ[:, fg * 512:(fg + 1) * 512], self.onesrow_bf[:],
                     self.brows[0:1, fg * 512:(fg + 1) * 512], start=False, stop=True)
            Tn.o("nop", inc=[(tc, 1)])
            ev[("qmm", i)] = tc.c

            for fg in range(4):
                V.w(tc, ev[("qmm", i)])
                V.o("tensor_scalar", escr[:, 0], psQ[:, fg * 512:(fg + 1) * 512],
                    sm[:, 0:1], None, op0=ALU.min, inc=[(vc, 1)])
                ev[("qmin", i, fg)] = vc.c
                S.w(vc, ev[("qmin", i, fg)])
                S.o("activation", out=escr[:, 1], in_=escr[:, 0], func=AF.Exp, inc=[(sc, 1)])
                ev[("qexp", i, fg)] = sc.c
                V.w(sc, ev[("qexp", i, fg)])
                V.o("scalar_tensor_tensor", qtm[:, fg * 512:(fg + 1) * 512],
                    psQ[:, fg * 512:(fg + 1) * 512], sm[:, 0:1], escr[:, 1],
                    op0=ALU.max, op1=ALU.add, inc=[(vc, 1)])
                ev[("kelu", i, fg)] = vc.c

            V.w(vc, ev["ksrep"])
            V.o("tensor_tensor", tmpf[:], qtm[:], ksrep[:], op=ALU.mult, inc=[(vc, 1)])
            V.w(vc)
            V.o("tensor_reduce", dn16[:],
                tmpf[:].rearrange("p (h d) -> p h d", h=16), axis=AX.X, op=ALU.add,
                inc=[(vc, 1)])
            ev[("dn", i)] = vc.c
            S.w(vc, ev[("dn", i)])
            S.o("activation", out=rcp16[:], in_=dn16[:], func=AF.Ln, bias=sm[:, 3:4],
                inc=[(sc, 1)])
            S.w(sc)
            S.o("activation", out=rcp16[:], in_=rcp16[:], func=AF.Exp, scale=-1.0,
                inc=[(sc, 1)])
            ev[("rcp", i)] = sc.c
            V.w(sc, ev[("rcp", i)])
            if i > 0:
                V.w(tc, ev[("qptr", i - 1, 1)])
            for h in range(16):
                V.o("tensor_scalar", qptm[:, h * 128:(h + 1) * 128],
                    qtm[:, h * 128:(h + 1) * 128], rcp16[:, h:h + 1], None,
                    op0=ALU.mult)
            V.o("nop", inc=[(vc, 1)])
            ev[("qp", i)] = vc.c

            for hf in range(2):
                Tn.w(vc, ev[("qp", i)])
                pk = ("qpev", i, 0) if hf == 1 else (("qpev", i - 1, 1) if i > 0 else None)
                if pk:
                    Tn.w(sc, ev[pk])
                for c in range(8):
                    cc = hf * 8 + c
                    Tn.o("transpose", psT[:, c * 128:(c + 1) * 128],
                         qptm[:, cc * 128:(cc + 1) * 128], self.ident_bf[:])
                Tn.o("nop", inc=[(tc, 1)])
                ev[("qptr", i, hf)] = tc.c
                S.w(tc, ev[("qptr", i, hf)])
                if i > 0 and hf == 0:
                    S.w(tc, ev[("qkv", i - 1)])
                S.o("activation", out=qpfm[:, hf * 1024:(hf + 1) * 1024],
                    in_=psT[:], func=AF.Copy, inc=[(sc, 1)])
                ev[("qpev", i, hf)] = sc.c

            Tn.w(sc, ev[("qpev", i, 1)])
            Tn.w(vc, ev[("kelu", i, 3)])
            Tn.w(vc, ev["kvbf"])
            Tn.w(vc, ev[("dn", i)])
            if i > 0:
                Tn.w(sc, ev[("oev", i - 1)])
            for h in range(16):
                Tn.o("matmul", psQ[:, h * 128:(h + 1) * 128],
                     kv_bf[:, h * 128:(h + 1) * 128],
                     qpfm[:, h * 128:(h + 1) * 128],
                     start=(h % 4 == 0), stop=(h % 4 == 3))
            Tn.o("nop", inc=[(tc, 1)])
            ev[("qkv", i)] = tc.c

            S.w(tc, ev[("qkv", i)])
            if i >= 2:
                S.w(*ev[("ostore", i - 2)])
            for g in range(4):
                S.o("activation", out=o_buf[:, i % 2, g * 512:(g + 1) * 512],
                    in_=psQ[:, g * 512:(g + 1) * 512], func=AF.Copy)
            S.o("nop", inc=[(sc, 1)])
            ev[("oev", i)] = sc.c

            qo = self.q("o%d" % (i % 2))
            Y.w(sc, ev[("oev", i)])
            Y.o("dma_start", out=self.o_hbm[i], in_=o_buf[:, i % 2], inc=[(qo, 16)])
            ev[("ostore", i)] = (qo, qo.c)

        self.ev_a2 = ev
        self.a2_end_tc = self.tc.c
        self.emit()
        es.close()

    def ph_b2(self):
        nc = self.nc
        es = contextlib.ExitStack()
        sb = lambda n, s, d: es.enter_context(nc.sbuf_tensor(n, s, d))
        T = self.T
        NTB = T // 512
        Y, G, Tn, V, S = self.Y, self.G, self.Tn, self.V, self.S
        sy, gp, tc, vc, sc = self.sy, self.gp, self.tc, self.vc, self.sc
        ev = {}
        evA2 = self.ev_a2

        wd_q = sb("wd_q", [128, IC, 512], BF16)
        wo_q = sb("wo_q", [128, DC, 512], BF16)
        db_q = sb("db_q", [128, 4, 512], BF16)
        h_blk = sb("h_blk", [128, 2, IC, 256], BF16)
        o_blk = sb("o_blk", [128, 4, D], BF16)
        zd_blk = sb("zd_blk", [128, 4, 512], BF16)
        x_blk = sb("x_blk", [128, 4, 512], F32)
        outst = sb("outst", [128, 2, 512], F32)
        bo_q = sb("bo_q", [1, 512], BF16)

        psY = es.enter_context(nc.psum_tensor("psY", [128, 1024], F32))

        yev_hist = []
        ystore_hist = []
        G.w(tc, self.a2_end_tc)
        Y.w(tc, self.a2_end_tc)
        for qtr in range(4):
            c0 = qtr * 512
            qbw = self.q("bw")
            if qtr > 0:
                G.w(tc, ev[("qtrdone", qtr - 1)])
            G.o("dma_start",
                out=wd_q[:],
                in_=self.wd_t[:, c0:c0 + 512].rearrange("(i p) o -> p i o", p=128),
                inc=[(qbw, 16)])
            G.o("dma_start",
                out=wo_q[:],
                in_=self.wo_t[:, c0:c0 + 512].rearrange("(c p) o -> p c o", p=128),
                inc=[(qbw, 16)])
            G.o("dma_start",
                out=db_q[:],
                in_=self.dBp_in[:, :, c0:c0 + 512].rearrange("e p o -> p e o"),
                inc=[(qbw, 16)])
            G.o("dma_start", out=bo_q[:], in_=self.brows_in[3:4, c0:c0 + 512],
                inc=[(qbw, 16)])
            ev[("wload", qtr)] = (qbw, qbw.c)

            for tb in range(NTB):
                t0 = tb * 512
                qb2 = self.q("b2")
                if qtr > 0 or tb > 0:
                    prev = ("tbdone", qtr, tb - 1) if tb > 0 else ("tbdone", qtr - 1, NTB - 1)
                    Y.w(tc, ev[prev])
                for hf in range(2):
                    Y.o("dma_start",
                        out=h_blk[:, hf],
                        in_=self.h_hbm[:, :, t0 + hf * 256:t0 + (hf + 1) * 256]
                        .rearrange("i p t -> p i t"), inc=[(qb2, 16)])
                Y.w(*evA2[("ostore", tb * 4 + 3)])
                Y.w(*evA2[("ostore", tb * 4 + 2)])
                Y.o("dma_start",
                    out=o_blk[:],
                    in_=self.o_hbm[tb * 4:(tb + 1) * 4].rearrange("b p f -> p b f"),
                    inc=[(qb2, 16)])
                Y.o("dma_start",
                    out=zd_blk[:],
                    in_=self.zd_hbm[:, :, t0:t0 + 512].rearrange("e p t -> p e t"),
                    inc=[(qb2, 16)])
                Y.o("dma_start",
                    out=x_blk[:],
                    in_=self.x[t0:t0 + 512, c0:c0 + 512]
                    .rearrange("(b p) f -> p b f", p=128), inc=[(qb2, 16)])
                ev[("bload", qtr, tb)] = (qb2, qb2.c)

                for ts4 in range(4):
                    yb = psY[:, (ts4 % 2) * 512:(ts4 % 2 + 1) * 512]
                    Tn.w(*ev[("bload", qtr, tb)])
                    Tn.w(*ev[("wload", qtr)])
                    if len(yev_hist) >= 2:
                        Tn.w(vc, yev_hist[-2])
                    hb_s = ts4 // 2
                    trng = (ts4 % 2) * 128
                    Tn.o("matmul", yb, self.onesrow_bf[:], bo_q[:],
                         start=True, stop=False)
                    for ic in range(IC - 1):
                        Tn.o("matmul", yb, h_blk[:, hb_s, ic, trng:trng + 128],
                             wd_q[:, ic, :], start=False, stop=False)
                    for e in (1, 3, 5, 7):
                        Tn.o("matmul", yb,
                             zd_blk[64:128, e // 2, ts4 * 128:(ts4 + 1) * 128],
                             db_q[64:128, e // 2, :], start=False, stop=False,
                             tile_position=(64, 0))
                    Tn.o("matmul", yb, h_blk[:, hb_s, IC - 1, trng:trng + 128],
                         wd_q[:, IC - 1, :], start=False, stop=False)
                    for c in range(DC):
                        Tn.o("matmul", yb, o_blk[:, ts4, c * 128:(c + 1) * 128],
                             wo_q[:, c, :], start=False, stop=False)
                    for e in (0, 2, 4, 6):
                        Tn.o("matmul", yb,
                             zd_blk[0:64, e // 2, ts4 * 128:(ts4 + 1) * 128],
                             db_q[0:64, e // 2, :], start=False, stop=(e == 6),
                             tile_position=(0, 0))
                    Tn.o("nop", inc=[(tc, 1)])
                    ev[("ymm", qtr, tb, ts4)] = tc.c

                    V.w(tc, ev[("ymm", qtr, tb, ts4)])
                    if len(ystore_hist) >= 2:
                        V.w(*ystore_hist[-2])
                    V.o("tensor_tensor", outst[:, ts4 % 2], yb, x_blk[:, ts4, :],
                        op=ALU.add, inc=[(vc, 1)])
                    ev[("yev", qtr, tb, ts4)] = vc.c
                    yev_hist.append(vc.c)

                    qst = self.q("st%d" % (ts4 % 2))
                    Y.w(vc, ev[("yev", qtr, tb, ts4)])
                    Y.o("dma_start",
                        out=self.yout[t0 + ts4 * 128:t0 + (ts4 + 1) * 128,
                                      c0:c0 + 512],
                        in_=outst[:, ts4 % 2], inc=[(qst, 16)])
                    ev[("ystore", qtr, tb, ts4)] = (qst, qst.c)
                    ystore_hist.append((qst, qst.c))
                ev[("tbdone", qtr, tb)] = tc.c
            ev[("qtrdone", qtr)] = tc.c

        self.ev_b2 = ev
        self.emit()
        es.close()

    def ph_final(self):
        self.tapit("kvp", self.kvp_hbm)
        self.tapit("ktv", self.ktv_hbm)
        self.tapit("smdbg", self.smdbg_hbm)
        self.tapit("xn", self.xn_hbm)
        self.tapit("h", self.h_hbm)
        self.tapit("o", self.o_hbm)
        self.tapit("zd", self.zd_hbm)
        self.tapit("kv", self.cc_out_t.ap())
        self.tapit("cw", self.cw_tm[:])
        Y = self.Y
        qtp = self.q("tp")
        for out, src_ in self.tapdefs:
            Y.o("dma_start", out=out, in_=src_, inc=[(qtp, 16)])
        for name, ctr in sorted(self._qsems.items()):
            Y.w(ctr, ctr.c)
        Y.w(self.sy, self.sy.c)
        self.G.w(self.gp, self.gp.c)
        self.emit()
        self.es.close()


def _host_prep(inputs):
    g = {k: np.asarray(v) for k, v in inputs.items()}
    TT_ = lambda a: np.ascontiguousarray(np.transpose(a))
    out = {}
    out["wq_t"] = TT_(g["L0_attn_wq"])
    out["wk_t"] = TT_(g["L0_attn_wk"])
    out["wv_t"] = TT_(g["L0_attn_wv"])
    out["wo_t"] = TT_(g["L0_attn_wo"])
    out["brows_in"] = np.ascontiguousarray(np.stack(
        [g["L0_attn_bq"], g["L0_attn_bk"], g["L0_attn_bv"], g["L0_attn_bo"]]))
    out["confw_in"] = TT_(g["L0_ffn_conf_w"])
    out["confb_in"] = np.ascontiguousarray(g["L0_ffn_conf_b"][None, :])
    out["wealth_in"] = np.ascontiguousarray(g["L0_ffn_wealth"][None, :])
    out["wg_t"] = TT_(g["L0_ffn_base_gate_w"])
    out["wu_t"] = TT_(g["L0_ffn_base_up_w"])
    out["wd_t"] = TT_(g["L0_ffn_base_down_w"])
    zA = np.empty((D, E, 128), np.float32)
    for e in range(E):
        zA[:, e, 0:64] = g["L0_ffn_gA"][e].T
        zA[:, e, 64:128] = g["L0_ffn_uA"][e].T
    out["zA_in"] = zA
    guB = np.empty((E, 128, I), np.float32)
    for e in range(E):
        guB[e, 0:64] = g["L0_ffn_gB"][e].T
        guB[e, 64:128] = g["L0_ffn_uB"][e].T
    out["guB_in"] = guB
    dAp = np.empty((I, E, R), np.float32)
    for e in range(E):
        dAp[:, e, :] = g["L0_ffn_dA"][e].T
    out["dAp_in"] = dAp
    dBp = np.empty((4, 128, D), np.float32)
    for p_ in range(4):
        dBp[p_, 0:64] = g["L0_ffn_dB"][2 * p_].T
        dBp[p_, 64:128] = g["L0_ffn_dB"][2 * p_ + 1].T
    out["dBp_in"] = dBp
    cc = np.zeros((128, 4), np.float32)
    cc[:, 1] = 1.0
    cc[:, 2] = LN_EPS
    cc[:, 3] = AT_EPS
    out["cconst_in"] = cc
    out["ident_in"] = np.eye(128, dtype=np.float32)
    out["onesr_in"] = np.ones((1, 128), np.float32)
    out["onesc_in"] = np.ones((128, 1), np.float32)
    return out


def kernel(**inputs):
    x = np.asarray(inputs["L0_attn_x"])
    B, S, _ = x.shape
    ncores = 8
    Tcore = B * S // ncores
    shared = _host_prep(inputs)
    prog = Prog(T=Tcore)
    in_maps = []
    for c in range(ncores):
        b = c // 4
        qq = c % 4
        m = dict(shared)
        m["x"] = np.ascontiguousarray(x[b, qq * Tcore:(qq + 1) * Tcore, :])
        in_maps.append(m)
    res = run_bass_kernel_spmd(prog.nc, in_maps, core_ids=list(range(ncores)))
    out = np.empty((B, S, D), np.float32)
    for c in range(ncores):
        b = c // 4
        qq = c % 4
        out[b, qq * Tcore:(qq + 1) * Tcore, :] = res.results[c]["yout"]
    return out
